# version 22
# baseline (speedup 1.0000x reference)
import sys

if "/opt/trn_rl_repo" not in sys.path:
    sys.path.insert(0, "/opt/trn_rl_repo")

import numpy as np
import concourse.bass as bass
import concourse.mybir as mybir
from concourse.bass_utils import run_bass_kernel_spmd

F32 = mybir.dt.float32
BF16 = mybir.dt.bfloat16
I32 = mybir.dt.int32
AOT = mybir.AluOpType
AF = mybir.ActivationFunctionType
AXX = mybir.AxisListType.X

N_CORES = 8


class Cfg:
    def __init__(self, d=2048, inter=5504, seg=512, t_core=2048):
        self.D = d
        self.I = inter
        self.SEG = seg
        self.T = t_core
        self.NH = d // 128
        self.KC = d // 128
        self.IC = inter // 128
        self.NSEG = t_core // seg
        self.PAR = seg // 2
        self.QC = self.PAR // 128
        self.TC = t_core // 128
        self.NTS = seg // 128
        self.NDG = d // 512
        self.SCALE = 1.0 / np.sqrt(128.0)
        self.IC_PAD = ((self.IC + 7) // 8) * 8
        self.IPR = self.IC_PAD // 8
        self.DPR = inter // 8
        assert inter % 8 == 0
        assert d % 512 == 0 and inter % 128 == 0
        assert seg % 256 == 0 and t_core % seg == 0
        assert self.NTS <= 4


class Sem:
    def __init__(self, handle):
        self.h = handle
        self.n = 0


class Plan:
    ENGS = ("sync", "tensor", "vector", "scalar", "gpsimd")

    def __init__(self):
        self.ops = {e: [] for e in self.ENGS}

    def add(self, engine, fn, waits=(), incs=()):
        ws = [(s, v) for (s, v) in waits if v > 0]
        self.ops[engine].append((fn, ws, list(incs)))
        for s, k in incs:
            s.n += k


def build_program(cfg: Cfg, debug=False, p1_only=False):
    nc = bass.Bass("TRN2", target_bir_lowering=False, debug=False,
                   detect_race_conditions=False)
    D, I, T, SEG = cfg.D, cfg.I, cfg.T, cfg.SEG
    KC, IC, NSEG = cfg.KC, cfg.IC, cfg.NSEG
    QC, PAR, NH, TC, NTS, NDG = (cfg.QC, cfg.PAR, cfg.NH, cfg.TC, cfg.NTS,
                                 cfg.NDG)

    x = nc.dram_tensor("x", [T, D], F32, kind="ExternalInput").ap()
    lut = nc.dram_tensor("lut", [1, 16], F32, kind="ExternalInput").ap()
    IPR, DPR, IC_PAD = cfg.IPR, cfg.DPR, cfg.IC_PAD
    gidxT = nc.dram_tensor("gidxT", [D, IPR * 128], I32, kind="ExternalInput").ap()
    uidxT = nc.dram_tensor("uidxT", [D, IPR * 128], I32, kind="ExternalInput").ap()
    didxT = nc.dram_tensor("didxT", [DPR, D], I32, kind="ExternalInput").ap()
    y = nc.dram_tensor("y", [T, D], F32, kind="ExternalOutput").ap()
    cin_g = nc.dram_tensor("cin_g", [IPR * 128, KC * 128], BF16, kind="Internal").ap()
    cin_u = nc.dram_tensor("cin_u", [IPR * 128, KC * 128], BF16, kind="Internal").ap()
    cin_d = nc.dram_tensor("cin_d", [DPR, D], BF16, kind="Internal").ap()
    wg_ag = nc.dram_tensor("wg_ag", [IC_PAD * 128, KC * 128], BF16, kind="Internal").ap()
    wu_ag = nc.dram_tensor("wu_ag", [IC_PAD * 128, KC * 128], BF16, kind="Internal").ap()
    wd = nc.dram_tensor("wd", [I, D], BF16, kind="Internal").ap()
    s_hbm = nc.dram_tensor("s_hbm", [NSEG, 128, NTS * D], BF16,
                           kind="Internal").ap()

    ctx = []

    def sb(name, shape, dt):
        t = nc.sbuf_tensor(name, shape, dt)
        ctx.append(t)
        return t.__enter__()

    def psum(name, shape, dt):
        t = nc.psum_tensor(name, shape, dt)
        ctx.append(t)
        return t.__enter__()

    def sem(name):
        s = nc.semaphore(name)
        ctx.append(s)
        return Sem(s.__enter__())

    xn = sb("xn", [128, KC * T], BF16)
    h_c = sb("h_c", [128, IC * SEG], BF16)
    s_sg = sb("s_sg", [128, NTS * D], BF16)
    x_tm = sb("x_tm", [128, 2 * D], F32)
    x_ep = sb("x_ep", [128, 6 * 512], F32)
    panel = sb("panel", [128, 2 * 2 * KC * 128], BF16)
    wd_sb = sb("wd_sb", [128, 4 * 512], BF16)
    sil = sb("sil", [128, 2 * SEG], F32)
    gd = sb("gd", [128, 2 * 512], F32)
    out_sb = sb("out_sb", [128, 2 * 512], F32)
    ident = sb("ident", [128, 128], BF16)
    lut_sb = sb("lut_sb", [1, 16], F32)
    lut_bc = sb("lut_bc", [128, 16], F32)
    ones1 = sb("ones1", [1, 128], F32)
    nst = (D + 511) // 512
    stats = sb("stats", [128, 6 * nst], F32)
    aggr = sb("aggr", [128, 2 * 2], F32)
    rstd = sb("rstd", [128, 2], F32)
    negmean = sb("negmean", [128, 2], F32)
    negmr = sb("negmr", [128, 2], F32)
    xn_tm = sb("xn_tm", [128, D], BF16)
    eT = sb("eT", [128, QC * PAR], BF16)
    gts = sb("gts", [128, QC * 128], BF16)
    zsum = sb("zsum", [128, QC], F32)
    lnz = sb("lnz", [128, QC], F32)
    rz = sb("rz", [128, QC], F32)
    cm30 = sb("cm30", [128, 1], F32)
    sring = sb("sring", [128, 4 * 512], BF16)
    DCW = 768
    dec_idx = sb("dec_idx", [128, 2 * DCW], I32)
    dec_b = sb("dec_b", [128, DCW], BF16)
    dec_t = sb("dec_t", [128, DCW], BF16)
    dec_w = sb("dec_w", [128, 3 * DCW], BF16)

    pt0 = psum("pt0", [128, 128], BF16)
    pt1 = psum("pt1", [128, 128], BF16)
    pa0 = psum("pa0", [128, 512], F32)
    pa1 = psum("pa1", [128, 512], F32)
    pb0 = psum("pb0", [128, 512], F32)
    pb1 = psum("pb1", [128, 512], F32)
    pc0 = psum("pc0", [128, 512], F32)
    pc1 = psum("pc1", [128, 512], F32)
    PT = (pt0, pt1)
    PA = (pa0, pa1)
    PB = (pb0, pb1)
    PC = (pc0, pc1)
    PAIRS = ((pa0, pa1), (pb0, pb1), (pc0, pc1))

    names = ("s_xld s_ones s_lnv s_lns s_tr s_trc s_lut s_lutb s_lutv "
             "s_dec s_att_pe s_att_v s_att_a s_gu_pe s_sil s_h "
             "s_dn_pe s_gd s_ep s_ms").split()
    SM = {n: sem(n) for n in names}
    (s_xld, s_ones, s_lnv, s_lns, s_tr, s_trc, s_lut, s_lutb, s_lutv,
     s_dec, s_att_pe, s_att_v, s_att_a, s_gu_pe, s_sil, s_h,
     s_dn_pe, s_gd, s_ep, s_ms) = (SM[n] for n in names)
    s_idx = [sem(f"s_idx{k}") for k in range(2)]
    s_wdma = [sem(f"s_wdma{k}") for k in range(3)]
    s_pan = [sem(f"s_pan{k}") for k in range(2)]
    s_wd = [sem(f"s_wd{k}") for k in range(4)]
    s_xep = [sem(f"s_xep{k}") for k in range(6)]
    s_out = [sem(f"s_out{k}") for k in range(2)]
    s_ag = sem("s_ag")
    s_sdump = sem("s_sdump")
    s_sr = [sem(f"s_sr{k}") for k in range(4)]

    def seg_rows(c, tchunk):
        half = SEG // 2
        par = (tchunk * 128) // half
        off = (tchunk * 128) % half
        start = c * SEG + par + 2 * off
        stop = min(start + 2 * 128, (c + 1) * SEG)
        return start, stop

    P = Plan()

    P.add("sync", lambda e: e.dma_start(out=lut_sb[:], in_=lut[:, :]),
          incs=[(s_lut, 16)])
    iot_v = dec_idx[:, 0:128]
    P.add("gpsimd", lambda e: e.iota(iot_v, pattern=[[1, 128]], base=0,
                                     channel_multiplier=-1))
    P.add("gpsimd", lambda e: e.drain())
    P.add("gpsimd", lambda e: e.tensor_scalar(out=ident[:], in0=iot_v,
                                              scalar1=0.0, scalar2=None,
                                              op0=AOT.is_equal),
          incs=[(s_ms, 1)])
    P.add("gpsimd", lambda e: e.memset(s_sg[:], 0.0), incs=[(s_ms, 1)])
    P.add("vector", lambda e: e.memset(ones1[:], 1.0), incs=[(s_ones, 1)])
    P.add("vector", lambda e: e.memset(cm30[:], -30.0), incs=[(s_ones, 1)])
    P.add("vector", lambda e: e.memset(cm30[:], -30.0), incs=[(s_ones, 1)])
    P.add("tensor",
          lambda e: e.matmul(pa0[:, 0:16], ones1[:], lut_sb[:],
                             start=True, stop=True),
          waits=[(s_lut, 16), (s_ones, 1)], incs=[(s_lutb, 1)])
    P.add("vector", lambda e: e.tensor_copy(lut_bc[:], pa0[:, 0:16]),
          waits=[(s_lutb, 1)], incs=[(s_lutv, 1)])

    for tc in range(TC):
        c, tch = tc // NTS, tc % NTS
        start, stop = seg_rows(c, tch)
        sl = tc % 2
        xsl = x_tm[:, sl * D:(sl + 1) * D]
        P.add("sync",
              lambda e, xsl=xsl, start=start, stop=stop: e.dma_start(
                  out=xsl, in_=x[start:stop:2, :]),
              waits=[(s_lns, 3 * (tc - 1))] if tc >= 2 else [],
              incs=[(s_xld, 16)])

        def ln(e, xsl=xsl, sl=sl):
            for j in range(nst):
                lo, hi = j * 512, min((j + 1) * 512, D)
                e.bn_stats(stats[:, j * 6:(j + 1) * 6], xsl[:, lo:hi])
            e.drain()
            e.bn_aggr(aggr[:, 2 * sl:2 * sl + 2], stats[:, 0:6 * nst])
            e.drain()
            e.tensor_scalar(out=negmean[:, sl:sl + 1],
                            in0=aggr[:, 2 * sl + 1:2 * sl + 2], scalar1=1e-5,
                            scalar2=None, op0=AOT.add)
            e.drain()
            e.reciprocal(rstd[:, sl:sl + 1], negmean[:, sl:sl + 1])
            e.drain()
            e.tensor_scalar(out=negmean[:, sl:sl + 1],
                            in0=aggr[:, 2 * sl:2 * sl + 1],
                            scalar1=-1.0, scalar2=None, op0=AOT.mult)
            return e.drain()
        lnw = [(s_xld, 16 * (tc + 1))]
        if tc >= 2:
            lnw.append((s_lns, 3 * (tc - 1)))
        P.add("vector", ln, waits=lnw, incs=[(s_lnv, 1)])
        P.add("scalar",
              lambda e, sl=sl: e.sqrt(out=rstd[:, sl:sl + 1],
                                      in_=rstd[:, sl:sl + 1]),
              waits=[(s_lnv, s_lnv.n)], incs=[(s_lns, 1)])

        def negmr_op(e, sl=sl):
            e.drain()
            e.activation(out=negmr[:, sl:sl + 1], in_=negmean[:, sl:sl + 1],
                         func=AF.Identity, scale=rstd[:, sl:sl + 1])
            return e.drain()
        P.add("scalar", negmr_op, incs=[(s_lns, 1)])

        def aff_op(e, xsl=xsl, sl=sl):
            e.activation(out=xn_tm[:], in_=xsl,
                         func=AF.Identity, bias=negmr[:, sl:sl + 1],
                         scale=rstd[:, sl:sl + 1])
            return e.drain()
        P.add("scalar", aff_op,
              waits=[(s_tr, KC * tc)] if tc >= 1 else [],
              incs=[(s_lns, 1)])
        for k in range(KC):
            n = tc * KC + k
            tw = [(s_lns, 3 * tc + 3)]
            if n >= 2:
                tw.append((s_trc, n - 1))
            P.add("tensor",
                  lambda e, k=k: e.transpose(
                      PT[k % 2][:, 0:128],
                      xn_tm[:, k * 128:(k + 1) * 128],
                      ident[:]),
                  waits=tw, incs=[(s_tr, 1)])
            P.add("scalar",
                  lambda e, k=k, tc=tc: e.copy(
                      out=xn[:, k * T + tc * 128:k * T + (tc + 1) * 128],
                      in_=PT[k % 2][:, 0:128]),
                  waits=[(s_tr, s_tr.n)], incs=[(s_trc, 1)])
    xn_ready = TC * KC

    if p1_only:
        dbg_xn = nc.dram_tensor("dbg_xn", [128, KC * T], BF16,
                                kind="ExternalOutput").ap()

        def dump1(e):
            e.wait_ge(s_trc.h, TC * KC)
            e.dma_start(out=dbg_xn[:, :], in_=xn[:]).then_inc(s_out[0].h, 16)
            e.wait_ge(s_out[0].h, 16)
            return None
        P.add("scalar", dump1)
        return _emit_and_close(nc, P, ctx)


    dec_blocks = []
    for kb in range(KC):
        dec_blocks.append(("g", kb, 128, IPR * 128))
    for kb in range(KC):
        dec_blocks.append(("u", kb, 128, IPR * 128))
    for tb in range((DPR + 127) // 128):
        for c0 in range(0, D, DCW):
            dec_blocks.append(("d", (tb, c0), min(128, DPR - tb * 128),
                               min(DCW, D - c0)))
    NB = len(dec_blocks)

    def emit_dec(bl):
        kind, b, rows, cols = dec_blocks[bl]
        if kind == "d":
            tb, c0 = b
            srcap = didxT[tb * 128:tb * 128 + rows, c0:c0 + cols]
        else:
            srcap = {"g": gidxT, "u": uidxT}[kind][b * 128:b * 128 + rows, 0:cols]
        iw = [(s_dec, bl - 1)] if bl >= 2 else []
        if bl == 0:
            iw.append((s_ms, 1))
        P.add("sync",
              lambda e, srcap=srcap, bl=bl, rows=rows, cols=cols: e.dma_start(
                  out=dec_idx[0:rows, (bl % 2) * DCW:(bl % 2) * DCW + cols],
                  in_=srcap),
              waits=iw, incs=[(s_idx[bl % 2], 16)])

        def dec(e, bl=bl, rows=rows, cols=cols):
            isl = dec_idx[0:rows, (bl % 2) * DCW:(bl % 2) * DCW + cols]
            wsl = dec_w[0:rows, (bl % 3) * DCW:(bl % 3) * DCW + cols]
            e.tensor_copy(dec_b[0:rows, 0:cols], isl)
            inst = None
            for vv in range(16):
                dst = wsl if vv == 0 else dec_t[0:rows, 0:cols]
                ts = e.tensor_scalar(out=dst, in0=dec_b[0:rows, 0:cols],
                                     scalar1=float(vv),
                                     scalar2=lut_bc[0:rows, vv:vv + 1],
                                     op0=AOT.is_equal, op1=AOT.mult)
                inst = ts if vv == 0 else e.tensor_tensor(
                    out=wsl, in0=wsl, in1=dec_t[0:rows, 0:cols], op=AOT.add)
            return inst
        dw = [(s_idx[bl % 2], 16 * (bl // 2 + 1)), (s_lutv, 1)]
        if bl >= 3:
            dw.append((s_wdma[bl % 3], 16 * (bl // 3)))
        P.add("vector", dec, waits=dw, incs=[(s_dec, 1)])

        if kind in ("g", "u"):
            cin = cin_g if kind == "g" else cin_u

            def wout(e, bl=bl, b=b, cin=cin):
                src_ap = dec_w[:, (bl % 3) * DCW:(bl % 3) * DCW + IPR * 128]
                return e.dma_start(
                    out=cin.rearrange("(i p) f -> i p f", p=128)
                        [:, :, b * 128:(b + 1) * 128]
                        .rearrange("i p j -> p i j"),
                    in_=src_ap.rearrange("p (i j) -> p i j", j=128))
            P.add("scalar", wout, waits=[(s_dec, s_dec.n)],
                  incs=[(s_wdma[bl % 3], 16)])
        else:
            def wout(e, bl=bl, b=b, rows=rows, cols=cols):
                tb, c0 = b
                return e.dma_start(
                    out=cin_d[tb * 128:tb * 128 + rows, c0:c0 + cols],
                    in_=dec_w[0:rows, (bl % 3) * DCW:(bl % 3) * DCW + cols])
            P.add("scalar", wout, waits=[(s_dec, s_dec.n)],
                  incs=[(s_wdma[bl % 3], 16)])

    NHEADS = NSEG * NH
    dec_done = 0
    for c in range(NSEG):
        grp_xn = (s_trc, KC * NTS * (c + 1))
        for hh in range(NH):
            p = hh % 2
            col0 = c * SEG + p * PAR
            a0, b0 = s_att_pe.n, s_att_a.n
            for qc in range(QC):
                sw = []
                if hh == 0:
                    sw.append(grp_xn)
                    if c == 0:
                        sw.append((s_lutv, 1))
                prev_drain = b0 - (2 + QC) + 1
                if prev_drain > 0:
                    sw.append((s_att_a, prev_drain))
                P.add("tensor",
                      lambda e, hh=hh, qc=qc, col0=col0: e.matmul(
                          PA[qc % 2][:, 0:PAR],
                          xn[:, hh * T + col0 + qc * 128:
                             hh * T + col0 + (qc + 1) * 128],
                          xn[:, hh * T + col0:hh * T + col0 + PAR],
                          start=True, stop=True),
                      waits=sw, incs=[(s_att_pe, 1)])
            for qc in range(QC):
                P.add("scalar",
                      lambda e, qc=qc: e.activation(
                          out=eT[:, qc * PAR:(qc + 1) * PAR],
                          in_=PA[qc % 2][:, 0:PAR], func=AF.Exp,
                          scale=cfg.SCALE, bias=cm30[:, 0:1],
                          accum_out=zsum[:, qc:qc + 1]),
                      waits=[(s_att_pe, a0 + qc + 1)]
                      + ([(s_ones, 3)] if (c == 0 and hh == 0) else []),
                      incs=[])
            P.add("scalar", lambda e: e.drain(), incs=[(s_att_a, 1)])
            for kc in range(QC):
                tw = [(s_trc, TC * KC)] if (c == 0 and hh == 0) else \
                     [(s_att_a, max(0, b0 - 1))]
                P.add("tensor",
                      lambda e, hh=hh, kc=kc, col0=col0: e.transpose(
                          PT[kc % 2][:, 0:128],
                          xn[:, hh * T + col0 + kc * 128:
                             hh * T + col0 + (kc + 1) * 128],
                          ident[:]),
                      waits=tw, incs=[(s_tr, 1)])
            def gcp(e):
                for kc in range(QC):
                    e.copy(out=gts[:, kc * 128:(kc + 1) * 128],
                           in_=PT[kc % 2][:, 0:128])
                return e.drain()
            P.add("scalar", gcp, waits=[(s_tr, s_tr.n)],
                  incs=[(s_att_a, QC)])
            for qc in range(QC):
                def omm(e, qc=qc):
                    inst = None
                    for kc in range(QC):
                        inst = e.matmul(PB[qc % 2][:, 0:128],
                                        eT[:, kc * PAR + qc * 128:
                                           kc * PAR + (qc + 1) * 128],
                                        gts[:, kc * 128:(kc + 1) * 128],
                                        start=(kc == 0), stop=(kc == QC - 1))
                    return inst
                P.add("tensor", omm, waits=[(s_att_a, b0 + 1 + QC)],
                      incs=[(s_att_pe, 1)])

            def rzc(e):
                for qc in range(QC):
                    e.activation(out=lnz[:, qc:qc + 1], in_=zsum[:, qc:qc + 1],
                                 func=AF.Ln)
                e.drain()
                for qc in range(QC):
                    e.activation(out=rz[:, qc:qc + 1], in_=lnz[:, qc:qc + 1],
                                 func=AF.Exp, scale=-1.0)
                return e.drain()
            P.add("scalar", rzc)
            for qc in range(QC):
                t_idx = p * (NTS // 2) + qc
                sww = [(s_att_pe, a0 + QC + qc + 1), (s_ms, c + 2)]
                P.add("scalar",
                      lambda e, qc=qc, t_idx=t_idx, hh=hh: e.activation(
                          out=s_sg[:, t_idx * D + hh * 128:
                                   t_idx * D + (hh + 1) * 128],
                          in_=PB[qc % 2][:, 0:128], func=AF.Identity,
                          scale=rz[:, qc:qc + 1]),
                      waits=sww, incs=[])
            P.add("scalar", lambda e: e.drain(), incs=[(s_att_a, 1)])
            hk = c * NH + hh + 1
            while dec_done < min(NB, (hk * NB + NHEADS - 1) // NHEADS):
                emit_dec(dec_done)
                dec_done += 1
        P.add("scalar",
              lambda e, c=c: e.dma_start(out=s_hbm[c, :, :], in_=s_sg[:]),
              waits=[(s_att_a, s_att_a.n)], incs=[(s_sdump, 16)])
        if c + 1 < NSEG:
            P.add("gpsimd", lambda e: e.memset(s_sg[:], 0.0),
                  waits=[(s_sdump, 16 * (c + 1))], incs=[(s_ms, 1)])
    A_END = s_att_a.n
    while dec_done < NB:
        emit_dec(dec_done)
        dec_done += 1

    ag_specs = [(cin_g, wg_ag, 0, KC),
                (cin_u, wu_ag, 0, 2 * KC),
                (cin_d, wd, 0, NB)]
    for cin, cout, _, nb_end in ag_specs:
        agw = [(s_wdma[k], 16 * len(range(k, nb_end, 3))) for k in range(3)]
        P.add("gpsimd",
              lambda e, cin=cin, cout=cout: e.collective_compute(
                  "AllGather", AOT.bypass, replica_groups=[list(range(8))],
                  ins=[cin[:, :]], outs=[cout[:, :]]),
              waits=agw, incs=[(s_ag, 1)])


    if debug:
        _dbg_sall = nc.dram_tensor("dbg_sall", [NSEG, 128, NTS * D], BF16,
                                   kind="ExternalOutput").ap()
        _dbg_gd = nc.dram_tensor("dbg_gd", [NSEG * NDG * NTS, 128, 512], F32,
                                 kind="ExternalOutput").ap()

    bank_free = {}
    pan_n = 0
    wd_n = 0
    ep_n = 0
    for c in range(NSEG):
        sil_base = s_sil.n
        h_base = s_h.n
        for i in range(IC):
            half = 2 * KC * 128
            pw = [(s_gu_pe, 2 * (pan_n - 1))] if pan_n >= 2 else []
            if pan_n == 0:
                pw.append((s_ag, 2))
            P.add("sync",
                  lambda e, i=i, pan_n=pan_n: e.dma_start(
                      out=panel[:, (pan_n % 2) * half:
                                (pan_n % 2) * half + KC * 128],
                      in_=wg_ag[i * 128:(i + 1) * 128, :]),
                  waits=pw, incs=[(s_pan[pan_n % 2], 16)])
            P.add("sync",
                  lambda e, i=i, pan_n=pan_n: e.dma_start(
                      out=panel[:, (pan_n % 2) * half + KC * 128:
                                (pan_n % 2 + 1) * half],
                      in_=wu_ag[i * 128:(i + 1) * 128, :]),
                  incs=[(s_pan[pan_n % 2], 16)])
            base = (pan_n % 2) * half
            for gu in range(2):
                mw = [(s_pan[pan_n % 2], 32 * (pan_n // 2 + 1))]
                if gu == 0:
                    if i >= 2:
                        mw.append((s_sil, sil_base + i - 1))
                    elif c == 0:
                        mw.append((s_att_a, A_END))
                    else:
                        mw.append((s_gd, s_gd.n))
                else:
                    if i >= 2:
                        mw.append((s_h, h_base + i - 1))
                    elif c == 0:
                        mw.append((s_att_a, A_END))
                    else:
                        mw.append((s_gd, s_gd.n))
                pg = PA[i % 2] if gu == 0 else PC[i % 2]

                def gumm(e, gu=gu, base=base, pg=pg, c=c):
                    inst = None
                    for k in range(KC):
                        inst = e.matmul(
                            pg[:, 0:SEG],
                            panel[:, base + gu * KC * 128 + k * 128:
                                  base + gu * KC * 128 + (k + 1) * 128],
                            xn[:, k * T + c * SEG:k * T + (c + 1) * SEG],
                            start=(k == 0), stop=(k == KC - 1))
                    return inst
                P.add("tensor", gumm, waits=mw, incs=[(s_gu_pe, 1)])
                if gu == 0:
                    swl = [(s_gu_pe, s_gu_pe.n)]
                    if i >= 2:
                        swl.append((s_h, h_base + i - 1))
                    def silu_op(e, i=i):
                        e.activation(
                            out=sil[:, (i % 2) * SEG:(i % 2 + 1) * SEG],
                            in_=PA[i % 2][:, 0:SEG], func=AF.Silu)
                        return e.drain()
                    P.add("scalar", silu_op, waits=swl, incs=[(s_sil, 1)])
            P.add("vector",
                  lambda e, i=i: e.tensor_tensor(
                      out=h_c[:, i * SEG:(i + 1) * SEG],
                      in0=sil[:, (i % 2) * SEG:(i % 2 + 1) * SEG],
                      in1=PC[i % 2][:, 0:SEG], op=AOT.mult),
                  waits=[(s_sil, s_sil.n), (s_gu_pe, s_gu_pe.n)],
                  incs=[(s_h, 1)])
            pan_n += 1
        h_cnt_end = s_h.n
        sil_end = s_sil.n

        for dg in range(NDG):
            if NTS > 2:
                PSD = PAIRS[(2 * dg) % 3] + PAIRS[(2 * dg + 1) % 3]
            else:
                PSD = PAIRS[dg % 3][:NTS]
            bank_waits = [bank_free.get(id(pp), 0) for pp in PSD]
            dgn = c * NDG + dg
            xep_pending = []
            for t in range(NTS):
                start, stop = seg_rows(c, t)
                gt = dgn * NTS + t
                xep_pending.append((gt, dg, start, stop))
            for i in range(IC):
                if i == min(8, IC - 1):
                    for gt, dgx, start, stop in xep_pending:
                        P.add("sync",
                              lambda e, gt=gt, dgx=dgx, start=start, stop=stop:
                              e.dma_start(
                                  out=x_ep[:, (gt % 6) * 512:(gt % 6 + 1) * 512],
                                  in_=x[start:stop:2, dgx * 512:(dgx + 1) * 512]),
                              waits=[(s_ep, gt - 5)] if gt >= 6 else [],
                              incs=[(s_xep[gt % 6], 16)])
                        t_of = gt % NTS
                        srw = [(s_ep, gt - 3)] if gt >= 4 else []
                        if dg == 0 and t_of == 0:
                            srw.append((s_sdump, 16 * (c + 1)))
                        P.add("sync",
                              lambda e, gt=gt, dgx=dgx, c=c, t_of=t_of:
                              e.dma_start(
                                  out=sring[:, (gt % 4) * 512:(gt % 4 + 1) * 512],
                                  in_=s_hbm[c, :, t_of * D + dgx * 512:
                                            t_of * D + (dgx + 1) * 512]),
                              waits=srw, incs=[(s_sr[gt % 4], 16)])
                    xep_pending = []
                P.add("sync",
                      lambda e, i=i, dg=dg, wd_n=wd_n: e.dma_start(
                          out=wd_sb[:, (wd_n % 4) * 512:(wd_n % 4 + 1) * 512],
                          in_=wd[i * 128:(i + 1) * 128,
                                 dg * 512:(dg + 1) * 512]),
                      waits=([(s_dn_pe, wd_n - 3)] if wd_n >= 4 else [])
                      + ([(s_ag, 3)] if wd_n == 0 else []),
                      incs=[(s_wd[wd_n % 4], 16)])
                mw = [(s_wd[wd_n % 4], 16 * (wd_n // 4 + 1)), (s_h, h_cnt_end)]
                if i == 0:
                    mw.append((s_sil, sil_end))
                    if c == 0 and dg == 0:
                        mw.append((s_att_a, A_END))
                    bw = max(bank_waits)
                    if bw > 0:
                        mw.append((s_gd, bw))

                def dmm(e, i=i, wd_n=wd_n, PSD=PSD):
                    inst = None
                    for t in range(NTS):
                        inst = e.matmul(
                            PSD[t][:, 0:512],
                            h_c[:, i * SEG + t * 128:i * SEG + (t + 1) * 128],
                            wd_sb[:, (wd_n % 4) * 512:(wd_n % 4 + 1) * 512],
                            start=(i == 0), stop=(i == IC - 1))
                    return inst
                P.add("tensor", dmm, waits=mw, incs=[(s_dn_pe, 1)])
                wd_n += 1
            for t in range(NTS):
                gw = [(s_dn_pe, s_dn_pe.n)]
                if ep_n >= 2:
                    gw.append((s_ep, ep_n - 1))
                def gd_op(e, t=t, ep_n=ep_n, PSD=PSD):
                    e.activation(
                        out=gd[:, (ep_n % 2) * 512:(ep_n % 2 + 1) * 512],
                        in_=PSD[t][:, 0:512], func=AF.Silu)
                    return e.drain()
                P.add("scalar", gd_op, waits=gw, incs=[(s_gd, 1)])
                bank_free[id(PSD[t])] = s_gd.n
                if debug:
                    P.add("scalar",
                          lambda e, ep_n=ep_n: e.dma_start(
                              out=_dbg_gd[ep_n, :, :],
                              in_=gd[:, (ep_n % 2) * 512:(ep_n % 2 + 1) * 512]),
                          incs=[(s_wdma, 16)])
                gt = dgn * NTS + t
                vw = [(s_gd, s_gd.n),
                      (s_xep[gt % 6], 16 * (gt // 6 + 1)),
                      (s_sr[gt % 4], 16 * (gt // 4 + 1))]
                if ep_n >= 2:
                    vw.append((s_out[ep_n % 2], 16 * (ep_n // 2)))

                def ep(e, t=t, dg=dg, gt=gt, ep_n=ep_n):
                    ob = out_sb[:, (ep_n % 2) * 512:(ep_n % 2 + 1) * 512]
                    e.tensor_tensor(
                        out=ob,
                        in0=gd[:, (ep_n % 2) * 512:(ep_n % 2 + 1) * 512],
                        in1=sring[:, (gt % 4) * 512:(gt % 4 + 1) * 512],
                        op=AOT.mult)
                    return e.tensor_tensor(
                        out=ob, in0=ob,
                        in1=x_ep[:, (gt % 6) * 512:(gt % 6 + 1) * 512],
                        op=AOT.add)
                P.add("vector", ep, waits=vw, incs=[(s_ep, 1)])
                start, stop = seg_rows(c, t)
                P.add("gpsimd",
                      lambda e, start=start, stop=stop, dg=dg, ep_n=ep_n:
                      e.dma_start(
                          out=y[start:stop:2, dg * 512:(dg + 1) * 512],
                          in_=out_sb[:, (ep_n % 2) * 512:(ep_n % 2 + 1) * 512]),
                      waits=[(s_ep, s_ep.n)], incs=[(s_out[ep_n % 2], 16)])
                ep_n += 1

    final_out0, final_out1 = s_out[0].n, s_out[1].n

    if debug:
        dbg_xn = nc.dram_tensor("dbg_xn", [128, KC * T], BF16,
                                kind="ExternalOutput").ap()
        dbg_s = nc.dram_tensor("dbg_s", [128, NTS * D], BF16,
                               kind="ExternalOutput").ap()
        dbg_h = nc.dram_tensor("dbg_h", [128, IC * SEG], BF16,
                               kind="ExternalOutput").ap()
        dbg_lut = nc.dram_tensor("dbg_lut", [128, 16], F32,
                                 kind="ExternalOutput").ap()
        dbg_wd = nc.dram_tensor("dbg_wd", [I, D], BF16,
                                kind="ExternalOutput").ap()
        dbg_att = nc.dram_tensor("dbg_att", [128, 8], F32,
                                 kind="ExternalOutput").ap()
        dbg_eT = nc.dram_tensor("dbg_eT", [128, QC * PAR], BF16,
                                kind="ExternalOutput").ap()
        dbg_gts = nc.dram_tensor("dbg_gts", [128, QC * 128], BF16,
                                 kind="ExternalOutput").ap()

        def dump(e):
            e.wait_ge(s_out[0].h, final_out0)
            e.wait_ge(s_out[1].h, final_out1)
            for ap_o, ap_i in ((dbg_xn[:, :], xn[:]), (dbg_s[:, :], s_sg[:]),
                               (dbg_h[:, :], h_c[:]), (dbg_lut[:, :], lut_bc[:]),
                               (dbg_wd[:, :], wd[:, :]),
                               (dbg_att[:, 6:6 + QC], zsum[:, 0:QC]),
                               (dbg_eT[:, :], eT[:]), (dbg_gts[:, :], gts[:])):
                e.dma_start(out=ap_o, in_=ap_i).then_inc(s_out[0].h, 16)
            e.wait_ge(s_out[0].h, final_out0 + 128)
            return None
        P.add("scalar", dump)
        s_out[0].n += 128

    def fin(e):
        e.wait_ge(s_out[0].h, final_out0)
        e.wait_ge(s_out[1].h, final_out1)
        return None
    if not debug:
        P.add("scalar", fin)
    P.add("sync", fin)

    return _emit_and_close(nc, P, ctx)


def _emit_and_close(nc, P, ctx):
    def emit(eng, ops):
        for fn, waits, incs in ops:
            for smm, val in waits:
                eng.wait_ge(smm.h, val)
            inst = fn(eng)
            for smm, k in incs:
                assert inst is not None
                inst.then_inc(smm.h, k)

    with nc.Block() as block:
        @block.sync
        def _(e):
            emit(e, P.ops["sync"])

        @block.tensor
        def _(e):
            emit(e, P.ops["tensor"])

        @block.vector
        def _(e):
            emit(e, P.ops["vector"])

        @block.scalar
        def _(e):
            emit(e, P.ops["scalar"])

        @block.gpsimd
        def _(e):
            emit(e, P.ops["gpsimd"])

    for t in reversed(ctx):
        t.__exit__(None, None, None)
    return nc


_NC_CACHE = {}


def get_program():
    if "full" not in _NC_CACHE:
        _NC_CACHE["full"] = build_program(Cfg())
    return _NC_CACHE["full"]


def shard_idx(cfg, gate_idx, up_idx, down_idx):
    gT = np.asarray(gate_idx, dtype=np.int32).T
    uT = np.asarray(up_idx, dtype=np.int32).T
    dT = np.asarray(down_idx, dtype=np.int32).T
    w = cfg.IPR * 128
    shards = []
    for r in range(N_CORES):
        lo = r * w
        hi = min(lo + w, cfg.I)
        g = np.zeros((cfg.D, w), np.int32)
        u = np.zeros((cfg.D, w), np.int32)
        if hi > lo:
            g[:, 0:hi - lo] = gT[:, lo:hi]
            u[:, 0:hi - lo] = uT[:, lo:hi]
        d = np.ascontiguousarray(dT[r * cfg.DPR:(r + 1) * cfg.DPR, :])
        shards.append((np.ascontiguousarray(g), np.ascontiguousarray(u), d))
    return shards


def kernel(L0_attn_x, L0_ffn_lut, L0_ffn_gate_idx, L0_ffn_up_idx,
           L0_ffn_down_idx):
    cfg = Cfg()
    x = np.ascontiguousarray(np.asarray(L0_attn_x, dtype=np.float32)
                             .reshape(-1, cfg.D))
    lutv = np.ascontiguousarray(np.asarray(L0_ffn_lut, dtype=np.float32)
                                .reshape(1, 16))
    shards = shard_idx(cfg, L0_ffn_gate_idx, L0_ffn_up_idx, L0_ffn_down_idx)

    nc = get_program()
    in_maps = [{
        "x": np.ascontiguousarray(x[c * cfg.T:(c + 1) * cfg.T]),
        "lut": lutv, "gidxT": shards[c][0], "uidxT": shards[c][1],
        "didxT": shards[c][2],
    } for c in range(N_CORES)]
    res = run_bass_kernel_spmd(nc, in_maps, core_ids=list(range(N_CORES)))
    out = np.concatenate([r["y"] for r in res.results], axis=0)
    return out.reshape(np.asarray(L0_attn_x).shape)


# version 23 (final)
# speedup vs baseline: 1.0284x; 1.0284x over previous
import sys

if "/opt/trn_rl_repo" not in sys.path:
    sys.path.insert(0, "/opt/trn_rl_repo")

import numpy as np
import concourse.bass as bass
import concourse.mybir as mybir
from concourse.bass_utils import run_bass_kernel_spmd

F32 = mybir.dt.float32
BF16 = mybir.dt.bfloat16
I32 = mybir.dt.int32
AOT = mybir.AluOpType
AF = mybir.ActivationFunctionType
AXX = mybir.AxisListType.X

N_CORES = 8


class Cfg:
    def __init__(self, d=2048, inter=5504, seg=512, t_core=2048):
        self.D = d
        self.I = inter
        self.SEG = seg
        self.T = t_core
        self.NH = d // 128
        self.KC = d // 128
        self.IC = inter // 128
        self.NSEG = t_core // seg
        self.PAR = seg // 2
        self.QC = self.PAR // 128
        self.TC = t_core // 128
        self.NTS = seg // 128
        self.NDG = d // 512
        self.SCALE = 1.0 / np.sqrt(128.0)
        self.IC_PAD = ((self.IC + 7) // 8) * 8
        self.IPR = self.IC_PAD // 8
        self.DPR = inter // 8
        assert inter % 8 == 0
        assert d % 512 == 0 and inter % 128 == 0
        assert seg % 256 == 0 and t_core % seg == 0
        assert self.NTS <= 4


class Sem:
    def __init__(self, handle):
        self.h = handle
        self.n = 0


class Plan:
    ENGS = ("sync", "tensor", "vector", "scalar", "gpsimd")

    def __init__(self):
        self.ops = {e: [] for e in self.ENGS}

    def add(self, engine, fn, waits=(), incs=()):
        ws = [(s, v) for (s, v) in waits if v > 0]
        self.ops[engine].append((fn, ws, list(incs)))
        for s, k in incs:
            s.n += k


def build_program(cfg: Cfg, debug=False, p1_only=False):
    nc = bass.Bass("TRN2", target_bir_lowering=False, debug=False,
                   detect_race_conditions=False)
    D, I, T, SEG = cfg.D, cfg.I, cfg.T, cfg.SEG
    KC, IC, NSEG = cfg.KC, cfg.IC, cfg.NSEG
    QC, PAR, NH, TC, NTS, NDG = (cfg.QC, cfg.PAR, cfg.NH, cfg.TC, cfg.NTS,
                                 cfg.NDG)

    x = nc.dram_tensor("x", [T, D], F32, kind="ExternalInput").ap()
    lut = nc.dram_tensor("lut", [1, 16], F32, kind="ExternalInput").ap()
    IPR, DPR, IC_PAD = cfg.IPR, cfg.DPR, cfg.IC_PAD
    gidxT = nc.dram_tensor("gidxT", [D, IPR * 128], I32, kind="ExternalInput").ap()
    uidxT = nc.dram_tensor("uidxT", [D, IPR * 128], I32, kind="ExternalInput").ap()
    didxT = nc.dram_tensor("didxT", [DPR, D], I32, kind="ExternalInput").ap()
    y = nc.dram_tensor("y", [T, D], F32, kind="ExternalOutput").ap()
    cin_g = nc.dram_tensor("cin_g", [IPR * 128, KC * 128], BF16, kind="Internal").ap()
    cin_u = nc.dram_tensor("cin_u", [IPR * 128, KC * 128], BF16, kind="Internal").ap()
    cin_d = nc.dram_tensor("cin_d", [DPR, D], BF16, kind="Internal").ap()
    wg_ag = nc.dram_tensor("wg_ag", [IC_PAD * 128, KC * 128], BF16, kind="Internal").ap()
    wu_ag = nc.dram_tensor("wu_ag", [IC_PAD * 128, KC * 128], BF16, kind="Internal").ap()
    wd = nc.dram_tensor("wd", [I, D], BF16, kind="Internal").ap()
    s_hbm = nc.dram_tensor("s_hbm", [NSEG, 128, NTS * D], BF16,
                           kind="Internal").ap()

    ctx = []

    def sb(name, shape, dt):
        t = nc.sbuf_tensor(name, shape, dt)
        ctx.append(t)
        return t.__enter__()

    def psum(name, shape, dt):
        t = nc.psum_tensor(name, shape, dt)
        ctx.append(t)
        return t.__enter__()

    def sem(name):
        s = nc.semaphore(name)
        ctx.append(s)
        return Sem(s.__enter__())

    xn = sb("xn", [128, KC * T], BF16)
    h_c = sb("h_c", [128, IC * SEG], BF16)
    s_sg = sb("s_sg", [128, NTS * D], BF16)
    x_tm = sb("x_tm", [128, 2 * D], F32)
    x_ep = sb("x_ep", [128, 6 * 512], F32)
    panel = sb("panel", [128, 2 * 2 * KC * 128], BF16)
    wd_sb = sb("wd_sb", [128, 4 * 512], BF16)
    sil = sb("sil", [128, 2 * SEG], F32)
    gd = sb("gd", [128, 2 * 512], F32)
    out_sb = sb("out_sb", [128, 2 * 512], F32)
    ident = sb("ident", [128, 128], BF16)
    lut_sb = sb("lut_sb", [1, 16], F32)
    lut_bc = sb("lut_bc", [128, 16], F32)
    ones1 = sb("ones1", [1, 128], F32)
    nst = (D + 511) // 512
    stats = sb("stats", [128, 6 * nst], F32)
    aggr = sb("aggr", [128, 2 * 2], F32)
    rstd = sb("rstd", [128, 2], F32)
    negmean = sb("negmean", [128, 2], F32)
    negmr = sb("negmr", [128, 2], F32)
    xn_tm = sb("xn_tm", [128, D], BF16)
    eT = sb("eT", [128, QC * PAR], BF16)
    gts = sb("gts", [128, QC * 128], BF16)
    zsum = sb("zsum", [128, QC], F32)
    lnz = sb("lnz", [128, QC], F32)
    rz = sb("rz", [128, QC], F32)
    cm30 = sb("cm30", [128, 1], F32)
    sring = sb("sring", [128, 4 * 512], BF16)
    DCW = 768
    dec_idx = sb("dec_idx", [128, 2 * DCW], I32)
    dec_b = sb("dec_b", [128, DCW], BF16)
    dec_t = sb("dec_t", [128, DCW], BF16)
    dec_w = sb("dec_w", [128, 3 * DCW], BF16)

    pt0 = psum("pt0", [128, 128], BF16)
    pt1 = psum("pt1", [128, 128], BF16)
    pa0 = psum("pa0", [128, 512], F32)
    pa1 = psum("pa1", [128, 512], F32)
    pb0 = psum("pb0", [128, 512], F32)
    pb1 = psum("pb1", [128, 512], F32)
    pc0 = psum("pc0", [128, 512], F32)
    pc1 = psum("pc1", [128, 512], F32)
    PT = (pt0, pt1)
    PA = (pa0, pa1)
    PB = (pb0, pb1)
    PC = (pc0, pc1)
    PAIRS = ((pa0, pa1), (pb0, pb1), (pc0, pc1))

    names = ("s_xld s_ones s_lnv s_lns s_tr s_trc s_lut s_lutb s_lutv "
             "s_dec s_att_pe s_att_v s_att_a s_gu_pe s_sil s_h "
             "s_dn_pe s_gd s_ep s_ms").split()
    SM = {n: sem(n) for n in names}
    (s_xld, s_ones, s_lnv, s_lns, s_tr, s_trc, s_lut, s_lutb, s_lutv,
     s_dec, s_att_pe, s_att_v, s_att_a, s_gu_pe, s_sil, s_h,
     s_dn_pe, s_gd, s_ep, s_ms) = (SM[n] for n in names)
    s_idx = [sem(f"s_idx{k}") for k in range(2)]
    s_wdma = [sem(f"s_wdma{k}") for k in range(3)]
    s_pan = [sem(f"s_pan{k}") for k in range(2)]
    s_wd = [sem(f"s_wd{k}") for k in range(4)]
    s_xep = [sem(f"s_xep{k}") for k in range(6)]
    s_out = [sem(f"s_out{k}") for k in range(2)]
    s_ag = sem("s_ag")
    s_sdump = sem("s_sdump")
    s_sr = [sem(f"s_sr{k}") for k in range(4)]

    def seg_rows(c, tchunk):
        half = SEG // 2
        par = (tchunk * 128) // half
        off = (tchunk * 128) % half
        start = c * SEG + par + 2 * off
        stop = min(start + 2 * 128, (c + 1) * SEG)
        return start, stop

    P = Plan()

    P.add("sync", lambda e: e.dma_start(out=lut_sb[:], in_=lut[:, :]),
          incs=[(s_lut, 16)])
    iot_v = dec_idx[:, 0:128]
    P.add("gpsimd", lambda e: e.iota(iot_v, pattern=[[1, 128]], base=0,
                                     channel_multiplier=-1))
    P.add("gpsimd", lambda e: e.drain())
    P.add("gpsimd", lambda e: e.tensor_scalar(out=ident[:], in0=iot_v,
                                              scalar1=0.0, scalar2=None,
                                              op0=AOT.is_equal),
          incs=[(s_ms, 1)])
    P.add("gpsimd", lambda e: e.memset(s_sg[:], 0.0), incs=[(s_ms, 1)])
    P.add("vector", lambda e: e.memset(ones1[:], 1.0), incs=[(s_ones, 1)])
    P.add("vector", lambda e: e.memset(cm30[:], -30.0), incs=[(s_ones, 1)])
    P.add("vector", lambda e: e.memset(cm30[:], -30.0), incs=[(s_ones, 1)])
    P.add("tensor",
          lambda e: e.matmul(pa0[:, 0:16], ones1[:], lut_sb[:],
                             start=True, stop=True),
          waits=[(s_lut, 16), (s_ones, 1)], incs=[(s_lutb, 1)])
    P.add("vector", lambda e: e.tensor_copy(lut_bc[:], pa0[:, 0:16]),
          waits=[(s_lutb, 1)], incs=[(s_lutv, 1)])

    for tc in range(TC):
        c, tch = tc // NTS, tc % NTS
        start, stop = seg_rows(c, tch)
        sl = tc % 2
        xsl = x_tm[:, sl * D:(sl + 1) * D]
        P.add("sync",
              lambda e, xsl=xsl, start=start, stop=stop: e.dma_start(
                  out=xsl, in_=x[start:stop:2, :]),
              waits=[(s_lns, 3 * (tc - 1))] if tc >= 2 else [],
              incs=[(s_xld, 16)])

        def ln(e, xsl=xsl, sl=sl):
            for j in range(nst):
                lo, hi = j * 512, min((j + 1) * 512, D)
                e.bn_stats(stats[:, j * 6:(j + 1) * 6], xsl[:, lo:hi])
            e.drain()
            e.bn_aggr(aggr[:, 2 * sl:2 * sl + 2], stats[:, 0:6 * nst])
            e.drain()
            e.tensor_scalar(out=negmean[:, sl:sl + 1],
                            in0=aggr[:, 2 * sl + 1:2 * sl + 2], scalar1=1e-5,
                            scalar2=None, op0=AOT.add)
            e.drain()
            e.reciprocal(rstd[:, sl:sl + 1], negmean[:, sl:sl + 1])
            e.drain()
            e.tensor_scalar(out=negmean[:, sl:sl + 1],
                            in0=aggr[:, 2 * sl:2 * sl + 1],
                            scalar1=-1.0, scalar2=None, op0=AOT.mult)
            return e.drain()
        lnw = [(s_xld, 16 * (tc + 1))]
        if tc >= 2:
            lnw.append((s_lns, 3 * (tc - 1)))
        P.add("vector", ln, waits=lnw, incs=[(s_lnv, 1)])
        P.add("scalar",
              lambda e, sl=sl: e.sqrt(out=rstd[:, sl:sl + 1],
                                      in_=rstd[:, sl:sl + 1]),
              waits=[(s_lnv, s_lnv.n)], incs=[(s_lns, 1)])

        def negmr_op(e, sl=sl):
            e.drain()
            e.activation(out=negmr[:, sl:sl + 1], in_=negmean[:, sl:sl + 1],
                         func=AF.Identity, scale=rstd[:, sl:sl + 1])
            return e.drain()
        P.add("scalar", negmr_op, incs=[(s_lns, 1)])

        def aff_op(e, xsl=xsl, sl=sl):
            e.activation(out=xn_tm[:], in_=xsl,
                         func=AF.Identity, bias=negmr[:, sl:sl + 1],
                         scale=rstd[:, sl:sl + 1])
            return e.drain()
        P.add("scalar", aff_op,
              waits=[(s_tr, KC * tc)] if tc >= 1 else [],
              incs=[(s_lns, 1)])
        for k in range(KC):
            n = tc * KC + k
            tw = [(s_lns, 3 * tc + 3)]
            if n >= 2:
                tw.append((s_trc, n - 1))
            P.add("tensor",
                  lambda e, k=k: e.transpose(
                      PT[k % 2][:, 0:128],
                      xn_tm[:, k * 128:(k + 1) * 128],
                      ident[:]),
                  waits=tw, incs=[(s_tr, 1)])
            P.add("scalar",
                  lambda e, k=k, tc=tc: e.copy(
                      out=xn[:, k * T + tc * 128:k * T + (tc + 1) * 128],
                      in_=PT[k % 2][:, 0:128]),
                  waits=[(s_tr, s_tr.n)], incs=[(s_trc, 1)])
    xn_ready = TC * KC

    if p1_only:
        dbg_xn = nc.dram_tensor("dbg_xn", [128, KC * T], BF16,
                                kind="ExternalOutput").ap()

        def dump1(e):
            e.wait_ge(s_trc.h, TC * KC)
            e.dma_start(out=dbg_xn[:, :], in_=xn[:]).then_inc(s_out[0].h, 16)
            e.wait_ge(s_out[0].h, 16)
            return None
        P.add("scalar", dump1)
        return _emit_and_close(nc, P, ctx)


    dec_blocks = []
    for kb in range(KC):
        dec_blocks.append(("g", kb, 128, IPR * 128))
    for kb in range(KC):
        dec_blocks.append(("u", kb, 128, IPR * 128))
    for tb in range((DPR + 127) // 128):
        for c0 in range(0, D, DCW):
            dec_blocks.append(("d", (tb, c0), min(128, DPR - tb * 128),
                               min(DCW, D - c0)))
    NB = len(dec_blocks)

    def emit_dec(bl):
        kind, b, rows, cols = dec_blocks[bl]
        if kind == "d":
            tb, c0 = b
            srcap = didxT[tb * 128:tb * 128 + rows, c0:c0 + cols]
        else:
            srcap = {"g": gidxT, "u": uidxT}[kind][b * 128:b * 128 + rows, 0:cols]
        iw = [(s_dec, bl - 1)] if bl >= 2 else []
        if bl == 0:
            iw.append((s_ms, 1))
        P.add("sync",
              lambda e, srcap=srcap, bl=bl, rows=rows, cols=cols: e.dma_start(
                  out=dec_idx[0:rows, (bl % 2) * DCW:(bl % 2) * DCW + cols],
                  in_=srcap),
              waits=iw, incs=[(s_idx[bl % 2], 16)])

        def dec(e, bl=bl, rows=rows, cols=cols):
            isl = dec_idx[0:rows, (bl % 2) * DCW:(bl % 2) * DCW + cols]
            wsl = dec_w[0:rows, (bl % 3) * DCW:(bl % 3) * DCW + cols]
            e.tensor_copy(dec_b[0:rows, 0:cols], isl)
            inst = None
            for vv in range(16):
                dst = wsl if vv == 0 else dec_t[0:rows, 0:cols]
                ts = e.tensor_scalar(out=dst, in0=dec_b[0:rows, 0:cols],
                                     scalar1=float(vv),
                                     scalar2=lut_bc[0:rows, vv:vv + 1],
                                     op0=AOT.is_equal, op1=AOT.mult)
                inst = ts if vv == 0 else e.tensor_tensor(
                    out=wsl, in0=wsl, in1=dec_t[0:rows, 0:cols], op=AOT.add)
            return inst
        dw = [(s_idx[bl % 2], 16 * (bl // 2 + 1)), (s_lutv, 1)]
        if bl >= 3:
            dw.append((s_wdma[bl % 3], 16 * (bl // 3)))
        P.add("vector", dec, waits=dw, incs=[(s_dec, 1)])

        if kind in ("g", "u"):
            cin = cin_g if kind == "g" else cin_u

            def wout(e, bl=bl, b=b, cin=cin):
                src_ap = dec_w[:, (bl % 3) * DCW:(bl % 3) * DCW + IPR * 128]
                return e.dma_start(
                    out=cin.rearrange("(i p) f -> i p f", p=128)
                        [:, :, b * 128:(b + 1) * 128]
                        .rearrange("i p j -> p i j"),
                    in_=src_ap.rearrange("p (i j) -> p i j", j=128))
            P.add("scalar", wout, waits=[(s_dec, s_dec.n)],
                  incs=[(s_wdma[bl % 3], 16)])
        else:
            def wout(e, bl=bl, b=b, rows=rows, cols=cols):
                tb, c0 = b
                return e.dma_start(
                    out=cin_d[tb * 128:tb * 128 + rows, c0:c0 + cols],
                    in_=dec_w[0:rows, (bl % 3) * DCW:(bl % 3) * DCW + cols])
            P.add("scalar", wout, waits=[(s_dec, s_dec.n)],
                  incs=[(s_wdma[bl % 3], 16)])

    NHEADS = NSEG * NH
    dec_done = 0
    for c in range(NSEG):
        grp_xn = (s_trc, KC * NTS * (c + 1))
        for hh in range(NH):
            p = hh % 2
            col0 = c * SEG + p * PAR
            a0, b0 = s_att_pe.n, s_att_a.n
            for qc in range(QC):
                sw = []
                if hh == 0:
                    sw.append(grp_xn)
                    if c == 0:
                        sw.append((s_lutv, 1))
                prev_drain = b0 - (2 + QC) + 1
                if prev_drain > 0:
                    sw.append((s_att_a, prev_drain))
                P.add("tensor",
                      lambda e, hh=hh, qc=qc, col0=col0: e.matmul(
                          PA[qc % 2][:, 0:PAR],
                          xn[:, hh * T + col0 + qc * 128:
                             hh * T + col0 + (qc + 1) * 128],
                          xn[:, hh * T + col0:hh * T + col0 + PAR],
                          start=True, stop=True),
                      waits=sw, incs=[(s_att_pe, 1)])
            for qc in range(QC):
                P.add("scalar",
                      lambda e, qc=qc: e.activation(
                          out=eT[:, qc * PAR:(qc + 1) * PAR],
                          in_=PA[qc % 2][:, 0:PAR], func=AF.Exp,
                          scale=cfg.SCALE, bias=cm30[:, 0:1],
                          accum_out=zsum[:, qc:qc + 1]),
                      waits=[(s_att_pe, a0 + qc + 1)]
                      + ([(s_ones, 3)] if (c == 0 and hh == 0) else []),
                      incs=[])
            P.add("scalar", lambda e: e.drain(), incs=[(s_att_a, 1)])
            for kc in range(QC):
                tw = [(s_trc, TC * KC)] if (c == 0 and hh == 0) else \
                     [(s_att_a, max(0, b0 - 1))]
                P.add("tensor",
                      lambda e, hh=hh, kc=kc, col0=col0: e.transpose(
                          PT[kc % 2][:, 0:128],
                          xn[:, hh * T + col0 + kc * 128:
                             hh * T + col0 + (kc + 1) * 128],
                          ident[:]),
                      waits=tw, incs=[(s_tr, 1)])
            def gcp(e):
                for kc in range(QC):
                    e.copy(out=gts[:, kc * 128:(kc + 1) * 128],
                           in_=PT[kc % 2][:, 0:128])
                return e.drain()
            P.add("scalar", gcp, waits=[(s_tr, s_tr.n)],
                  incs=[(s_att_a, QC)])
            for qc in range(QC):
                def omm(e, qc=qc):
                    inst = None
                    for kc in range(QC):
                        inst = e.matmul(PB[qc % 2][:, 0:128],
                                        eT[:, kc * PAR + qc * 128:
                                           kc * PAR + (qc + 1) * 128],
                                        gts[:, kc * 128:(kc + 1) * 128],
                                        start=(kc == 0), stop=(kc == QC - 1))
                    return inst
                P.add("tensor", omm, waits=[(s_att_a, b0 + 1 + QC)],
                      incs=[(s_att_pe, 1)])

            def rzc(e):
                for qc in range(QC):
                    e.activation(out=lnz[:, qc:qc + 1], in_=zsum[:, qc:qc + 1],
                                 func=AF.Ln)
                e.drain()
                for qc in range(QC):
                    e.activation(out=rz[:, qc:qc + 1], in_=lnz[:, qc:qc + 1],
                                 func=AF.Exp, scale=-1.0)
                return e.drain()
            P.add("scalar", rzc)
            for qc in range(QC):
                t_idx = p * (NTS // 2) + qc
                sww = [(s_att_pe, a0 + QC + qc + 1), (s_ms, c + 2)]
                P.add("scalar",
                      lambda e, qc=qc, t_idx=t_idx, hh=hh: e.activation(
                          out=s_sg[:, t_idx * D + hh * 128:
                                   t_idx * D + (hh + 1) * 128],
                          in_=PB[qc % 2][:, 0:128], func=AF.Identity,
                          scale=rz[:, qc:qc + 1]),
                      waits=sww, incs=[])
            P.add("scalar", lambda e: e.drain(), incs=[(s_att_a, 1)])
            hk = c * NH + hh + 1
            while dec_done < min(NB, (hk * NB + NHEADS - 1) // NHEADS):
                emit_dec(dec_done)
                dec_done += 1
        P.add("scalar",
              lambda e, c=c: e.dma_start(out=s_hbm[c, :, :], in_=s_sg[:]),
              waits=[(s_att_a, s_att_a.n)], incs=[(s_sdump, 16)])
        if c + 1 < NSEG:
            P.add("vector", lambda e: e.memset(s_sg[:], 0.0),
                  waits=[(s_sdump, 16 * (c + 1))], incs=[(s_ms, 1)])
    A_END = s_att_a.n
    while dec_done < NB:
        emit_dec(dec_done)
        dec_done += 1

    ag_specs = [(cin_g, wg_ag, 0, KC),
                (cin_u, wu_ag, 0, 2 * KC),
                (cin_d, wd, 0, NB)]
    for cin, cout, _, nb_end in ag_specs:
        agw = [(s_wdma[k], 16 * len(range(k, nb_end, 3))) for k in range(3)]
        P.add("gpsimd",
              lambda e, cin=cin, cout=cout: e.collective_compute(
                  "AllGather", AOT.bypass, replica_groups=[list(range(8))],
                  ins=[cin[:, :]], outs=[cout[:, :]]),
              waits=agw, incs=[(s_ag, 1)])


    if debug:
        _dbg_sall = nc.dram_tensor("dbg_sall", [NSEG, 128, NTS * D], BF16,
                                   kind="ExternalOutput").ap()
        _dbg_gd = nc.dram_tensor("dbg_gd", [NSEG * NDG * NTS, 128, 512], F32,
                                 kind="ExternalOutput").ap()

    bank_free = {}
    pan_n = 0
    wd_n = 0
    ep_n = 0
    for c in range(NSEG):
        sil_base = s_sil.n
        h_base = s_h.n
        for i in range(IC):
            half = 2 * KC * 128
            pw = [(s_gu_pe, 2 * (pan_n - 1))] if pan_n >= 2 else []
            if pan_n == 0:
                pw.append((s_ag, 2))
            P.add("sync",
                  lambda e, i=i, pan_n=pan_n: e.dma_start(
                      out=panel[:, (pan_n % 2) * half:
                                (pan_n % 2) * half + KC * 128],
                      in_=wg_ag[i * 128:(i + 1) * 128, :]),
                  waits=pw, incs=[(s_pan[pan_n % 2], 16)])
            P.add("sync",
                  lambda e, i=i, pan_n=pan_n: e.dma_start(
                      out=panel[:, (pan_n % 2) * half + KC * 128:
                                (pan_n % 2 + 1) * half],
                      in_=wu_ag[i * 128:(i + 1) * 128, :]),
                  incs=[(s_pan[pan_n % 2], 16)])
            base = (pan_n % 2) * half
            for gu in range(2):
                mw = [(s_pan[pan_n % 2], 32 * (pan_n // 2 + 1))]
                if gu == 0:
                    if i >= 2:
                        mw.append((s_sil, sil_base + i - 1))
                    elif c == 0:
                        mw.append((s_att_a, A_END))
                    else:
                        mw.append((s_gd, s_gd.n))
                else:
                    if i >= 2:
                        mw.append((s_h, h_base + i - 1))
                    elif c == 0:
                        mw.append((s_att_a, A_END))
                    else:
                        mw.append((s_gd, s_gd.n))
                pg = PA[i % 2] if gu == 0 else PC[i % 2]

                def gumm(e, gu=gu, base=base, pg=pg, c=c):
                    inst = None
                    for k in range(KC):
                        inst = e.matmul(
                            pg[:, 0:SEG],
                            panel[:, base + gu * KC * 128 + k * 128:
                                  base + gu * KC * 128 + (k + 1) * 128],
                            xn[:, k * T + c * SEG:k * T + (c + 1) * SEG],
                            start=(k == 0), stop=(k == KC - 1))
                    return inst
                P.add("tensor", gumm, waits=mw, incs=[(s_gu_pe, 1)])
                if gu == 0:
                    swl = [(s_gu_pe, s_gu_pe.n)]
                    if i >= 2:
                        swl.append((s_h, h_base + i - 1))
                    def silu_op(e, i=i):
                        e.activation(
                            out=sil[:, (i % 2) * SEG:(i % 2 + 1) * SEG],
                            in_=PA[i % 2][:, 0:SEG], func=AF.Silu)
                        return e.drain()
                    P.add("scalar", silu_op, waits=swl, incs=[(s_sil, 1)])
            P.add("vector",
                  lambda e, i=i: e.tensor_tensor(
                      out=h_c[:, i * SEG:(i + 1) * SEG],
                      in0=sil[:, (i % 2) * SEG:(i % 2 + 1) * SEG],
                      in1=PC[i % 2][:, 0:SEG], op=AOT.mult),
                  waits=[(s_sil, s_sil.n), (s_gu_pe, s_gu_pe.n)],
                  incs=[(s_h, 1)])
            pan_n += 1
        h_cnt_end = s_h.n
        sil_end = s_sil.n

        for dg in range(NDG):
            if NTS > 2:
                PSD = PAIRS[(2 * dg) % 3] + PAIRS[(2 * dg + 1) % 3]
            else:
                PSD = PAIRS[dg % 3][:NTS]
            bank_waits = [bank_free.get(id(pp), 0) for pp in PSD]
            dgn = c * NDG + dg
            xep_pending = []
            for t in range(NTS):
                start, stop = seg_rows(c, t)
                gt = dgn * NTS + t
                xep_pending.append((gt, dg, start, stop))
            for i in range(IC):
                if i == min(8, IC - 1):
                    for gt, dgx, start, stop in xep_pending:
                        P.add("sync",
                              lambda e, gt=gt, dgx=dgx, start=start, stop=stop:
                              e.dma_start(
                                  out=x_ep[:, (gt % 6) * 512:(gt % 6 + 1) * 512],
                                  in_=x[start:stop:2, dgx * 512:(dgx + 1) * 512]),
                              waits=[(s_ep, gt - 5)] if gt >= 6 else [],
                              incs=[(s_xep[gt % 6], 16)])
                        t_of = gt % NTS
                        srw = [(s_ep, gt - 3)] if gt >= 4 else []
                        if dg == 0 and t_of == 0:
                            srw.append((s_sdump, 16 * (c + 1)))
                        P.add("sync",
                              lambda e, gt=gt, dgx=dgx, c=c, t_of=t_of:
                              e.dma_start(
                                  out=sring[:, (gt % 4) * 512:(gt % 4 + 1) * 512],
                                  in_=s_hbm[c, :, t_of * D + dgx * 512:
                                            t_of * D + (dgx + 1) * 512]),
                              waits=srw, incs=[(s_sr[gt % 4], 16)])
                    xep_pending = []
                P.add("sync",
                      lambda e, i=i, dg=dg, wd_n=wd_n: e.dma_start(
                          out=wd_sb[:, (wd_n % 4) * 512:(wd_n % 4 + 1) * 512],
                          in_=wd[i * 128:(i + 1) * 128,
                                 dg * 512:(dg + 1) * 512]),
                      waits=([(s_dn_pe, wd_n - 3)] if wd_n >= 4 else [])
                      + ([(s_ag, 3)] if wd_n == 0 else []),
                      incs=[(s_wd[wd_n % 4], 16)])
                mw = [(s_wd[wd_n % 4], 16 * (wd_n // 4 + 1)), (s_h, h_cnt_end)]
                if i == 0:
                    mw.append((s_sil, sil_end))
                    if c == 0 and dg == 0:
                        mw.append((s_att_a, A_END))
                    bw = max(bank_waits)
                    if bw > 0:
                        mw.append((s_gd, bw))

                def dmm(e, i=i, wd_n=wd_n, PSD=PSD):
                    inst = None
                    for t in range(NTS):
                        inst = e.matmul(
                            PSD[t][:, 0:512],
                            h_c[:, i * SEG + t * 128:i * SEG + (t + 1) * 128],
                            wd_sb[:, (wd_n % 4) * 512:(wd_n % 4 + 1) * 512],
                            start=(i == 0), stop=(i == IC - 1))
                    return inst
                P.add("tensor", dmm, waits=mw, incs=[(s_dn_pe, 1)])
                wd_n += 1
            for t in range(NTS):
                gw = [(s_dn_pe, s_dn_pe.n)]
                if ep_n >= 2:
                    gw.append((s_ep, ep_n - 1))
                def gd_op(e, t=t, ep_n=ep_n, PSD=PSD):
                    e.activation(
                        out=gd[:, (ep_n % 2) * 512:(ep_n % 2 + 1) * 512],
                        in_=PSD[t][:, 0:512], func=AF.Silu)
                    return e.drain()
                P.add("scalar", gd_op, waits=gw, incs=[(s_gd, 1)])
                bank_free[id(PSD[t])] = s_gd.n
                if debug:
                    P.add("scalar",
                          lambda e, ep_n=ep_n: e.dma_start(
                              out=_dbg_gd[ep_n, :, :],
                              in_=gd[:, (ep_n % 2) * 512:(ep_n % 2 + 1) * 512]),
                          incs=[(s_wdma, 16)])
                gt = dgn * NTS + t
                vw = [(s_gd, s_gd.n),
                      (s_xep[gt % 6], 16 * (gt // 6 + 1)),
                      (s_sr[gt % 4], 16 * (gt // 4 + 1))]
                if ep_n >= 2:
                    vw.append((s_out[ep_n % 2], 16 * (ep_n // 2)))

                def ep(e, t=t, dg=dg, gt=gt, ep_n=ep_n):
                    ob = out_sb[:, (ep_n % 2) * 512:(ep_n % 2 + 1) * 512]
                    e.tensor_tensor(
                        out=ob,
                        in0=gd[:, (ep_n % 2) * 512:(ep_n % 2 + 1) * 512],
                        in1=sring[:, (gt % 4) * 512:(gt % 4 + 1) * 512],
                        op=AOT.mult)
                    return e.tensor_tensor(
                        out=ob, in0=ob,
                        in1=x_ep[:, (gt % 6) * 512:(gt % 6 + 1) * 512],
                        op=AOT.add)
                P.add("vector", ep, waits=vw, incs=[(s_ep, 1)])
                start, stop = seg_rows(c, t)
                P.add("gpsimd",
                      lambda e, start=start, stop=stop, dg=dg, ep_n=ep_n:
                      e.dma_start(
                          out=y[start:stop:2, dg * 512:(dg + 1) * 512],
                          in_=out_sb[:, (ep_n % 2) * 512:(ep_n % 2 + 1) * 512]),
                      waits=[(s_ep, s_ep.n)], incs=[(s_out[ep_n % 2], 16)])
                ep_n += 1

    final_out0, final_out1 = s_out[0].n, s_out[1].n

    if debug:
        dbg_xn = nc.dram_tensor("dbg_xn", [128, KC * T], BF16,
                                kind="ExternalOutput").ap()
        dbg_s = nc.dram_tensor("dbg_s", [128, NTS * D], BF16,
                               kind="ExternalOutput").ap()
        dbg_h = nc.dram_tensor("dbg_h", [128, IC * SEG], BF16,
                               kind="ExternalOutput").ap()
        dbg_lut = nc.dram_tensor("dbg_lut", [128, 16], F32,
                                 kind="ExternalOutput").ap()
        dbg_wd = nc.dram_tensor("dbg_wd", [I, D], BF16,
                                kind="ExternalOutput").ap()
        dbg_att = nc.dram_tensor("dbg_att", [128, 8], F32,
                                 kind="ExternalOutput").ap()
        dbg_eT = nc.dram_tensor("dbg_eT", [128, QC * PAR], BF16,
                                kind="ExternalOutput").ap()
        dbg_gts = nc.dram_tensor("dbg_gts", [128, QC * 128], BF16,
                                 kind="ExternalOutput").ap()

        def dump(e):
            e.wait_ge(s_out[0].h, final_out0)
            e.wait_ge(s_out[1].h, final_out1)
            for ap_o, ap_i in ((dbg_xn[:, :], xn[:]), (dbg_s[:, :], s_sg[:]),
                               (dbg_h[:, :], h_c[:]), (dbg_lut[:, :], lut_bc[:]),
                               (dbg_wd[:, :], wd[:, :]),
                               (dbg_att[:, 6:6 + QC], zsum[:, 0:QC]),
                               (dbg_eT[:, :], eT[:]), (dbg_gts[:, :], gts[:])):
                e.dma_start(out=ap_o, in_=ap_i).then_inc(s_out[0].h, 16)
            e.wait_ge(s_out[0].h, final_out0 + 128)
            return None
        P.add("scalar", dump)
        s_out[0].n += 128

    def fin(e):
        e.wait_ge(s_out[0].h, final_out0)
        e.wait_ge(s_out[1].h, final_out1)
        return None
    if not debug:
        P.add("scalar", fin)
    P.add("sync", fin)

    return _emit_and_close(nc, P, ctx)


def _emit_and_close(nc, P, ctx):
    def emit(eng, ops):
        for fn, waits, incs in ops:
            for smm, val in waits:
                eng.wait_ge(smm.h, val)
            inst = fn(eng)
            for smm, k in incs:
                assert inst is not None
                inst.then_inc(smm.h, k)

    with nc.Block() as block:
        @block.sync
        def _(e):
            emit(e, P.ops["sync"])

        @block.tensor
        def _(e):
            emit(e, P.ops["tensor"])

        @block.vector
        def _(e):
            emit(e, P.ops["vector"])

        @block.scalar
        def _(e):
            emit(e, P.ops["scalar"])

        @block.gpsimd
        def _(e):
            emit(e, P.ops["gpsimd"])

    for t in reversed(ctx):
        t.__exit__(None, None, None)
    return nc


_NC_CACHE = {}


def get_program():
    if "full" not in _NC_CACHE:
        _NC_CACHE["full"] = build_program(Cfg())
    return _NC_CACHE["full"]


def shard_idx(cfg, gate_idx, up_idx, down_idx):
    gT = np.asarray(gate_idx, dtype=np.int32).T
    uT = np.asarray(up_idx, dtype=np.int32).T
    dT = np.asarray(down_idx, dtype=np.int32).T
    w = cfg.IPR * 128
    shards = []
    for r in range(N_CORES):
        lo = r * w
        hi = min(lo + w, cfg.I)
        g = np.zeros((cfg.D, w), np.int32)
        u = np.zeros((cfg.D, w), np.int32)
        if hi > lo:
            g[:, 0:hi - lo] = gT[:, lo:hi]
            u[:, 0:hi - lo] = uT[:, lo:hi]
        d = np.ascontiguousarray(dT[r * cfg.DPR:(r + 1) * cfg.DPR, :])
        shards.append((np.ascontiguousarray(g), np.ascontiguousarray(u), d))
    return shards


def kernel(L0_attn_x, L0_ffn_lut, L0_ffn_gate_idx, L0_ffn_up_idx,
           L0_ffn_down_idx):
    cfg = Cfg()
    x = np.ascontiguousarray(np.asarray(L0_attn_x, dtype=np.float32)
                             .reshape(-1, cfg.D))
    lutv = np.ascontiguousarray(np.asarray(L0_ffn_lut, dtype=np.float32)
                                .reshape(1, 16))
    shards = shard_idx(cfg, L0_ffn_gate_idx, L0_ffn_up_idx, L0_ffn_down_idx)

    nc = get_program()
    in_maps = [{
        "x": np.ascontiguousarray(x[c * cfg.T:(c + 1) * cfg.T]),
        "lut": lutv, "gidxT": shards[c][0], "uidxT": shards[c][1],
        "didxT": shards[c][2],
    } for c in range(N_CORES)]
    res = run_bass_kernel_spmd(nc, in_maps, core_ids=list(range(N_CORES)))
    out = np.concatenate([r["y"] for r in res.results], axis=0)
    return out.reshape(np.asarray(L0_attn_x).shape)
